# version 03 (final)
# baseline (speedup 1.0000x reference)
import math
import numpy as np
import ml_dtypes

import concourse.bass as bass
import concourse.mybir as mybir
from concourse.bass_utils import run_bass_kernel_spmd

F32 = mybir.dt.float32
BF16 = mybir.dt.bfloat16
U32 = mybir.dt.uint32
AF = mybir.ActivationFunctionType
ALU = mybir.AluOpType
AXX = mybir.AxisListType.X

B, N, D = 2, 8192, 2048
H, HD, CH = 16, 128, 128
SEG = 2048
FOR = 3 * SEG
NFC = FOR // CH
NTC = SEG // CH
E, I1, I2 = 64, 512, 2048
G, TOPG, TOPK, RSF = 8, 4, 6, 2.5
EPS = 1e-5

ENGS = ("PE", "DVE", "ACT", "POOL", "SP")


def _slopes(n_heads):
    start = 2.0 ** (-(2.0 ** (-(math.log2(n_heads) - 3))))
    return np.array([start ** (i + 1) for i in range(n_heads)], dtype=np.float64)


def _host_tables():
    lam = np.exp(-_slopes(H))
    i = np.arange(CH, dtype=np.float64)
    t = {}
    dm = np.zeros((CH, H * CH))
    dq = np.zeros((CH, H * CH))
    dktab = np.zeros((CH, H * CH))
    for h in range(H):
        diff = i[None, :] - i[:, None]
        dm[:, h * CH:(h + 1) * CH] = np.where(diff >= 0, lam[h] ** np.maximum(diff, 0.0), 0.0)
        dq[:, h * CH:(h + 1) * CH] = (lam[h] ** (i + 1.0))[None, :]
        dktab[:, h * CH:(h + 1) * CH] = (lam[h] ** (CH - 1.0 - i))[:, None]
    t["dmaskT"] = dm.astype(np.float32)
    t["dqtab"] = dq.astype(np.float32)
    t["dktab"] = dktab.astype(np.float32)
    dkcols = np.zeros((CH, H))
    for h in range(H):
        dkcols[:, h] = lam[h] ** (CH - 1.0 - i)
    t["dkcols"] = dkcols.astype(np.float32)
    lamc = (lam ** CH).astype(np.float32)
    t["lamc_cols"] = np.ascontiguousarray(np.broadcast_to(lamc[None, :], (CH, H)))
    fc = np.zeros((CH, 8), dtype=np.float32)
    fc[:, 0] = EPS
    fc[:, 1] = 1e-20
    fc[:, 2] = RSF
    fc[:, 3] = 1.0 / D
    t["fconst"] = fc
    t["idn_bf"] = np.eye(CH, dtype=np.float32)
    t["idn_f"] = np.eye(CH, dtype=np.float32)
    return t


SAFE_MODE = True

class Ctx:

    def __init__(self, engs, semh):
        self.engs = engs
        self.semh = semh
        self.cur = None
        self.cnt = {}

    def begin(self, cur):
        self.cur = cur
        self.cnt = {k: 0 for k in self.semh}

    def val(self, sem):
        return self.cnt[sem]

    def op(self, eng, fn, waits=(), incs=()):
        if SAFE_MODE:
            waits = tuple((s, None) for s in self.semh)
        rw = [(s, (self.cnt[s] if v is None else v)) for (s, v) in waits]
        if self.cur == eng:
            e = self.engs[eng]
            for s, v in rw:
                if v > 0:
                    e.wait_ge(self.semh[s], v)
            inst = fn(e)
            for s, k in incs:
                inst.then_inc(self.semh[s], k)
        for s, k in incs:
            self.cnt[s] += k

    def dma(self, eng, out, in_, sem, waits=()):
        self.op(eng, lambda e: e.dma_start(out=out, in_=in_), waits=waits,
                incs=((sem, 16),))

    def wait(self, eng, sem, v=None):
        if self.cur == eng:
            self.engs[eng].wait_ge(self.semh[sem], self.cnt[sem] if v is None else v)


class Arena:
    def __init__(self, tensor, cols, name):
        self.t, self.cols, self.name = tensor, cols, name
        self.off = 0
        self.marks = []

    def alloc(self, ncols, tag=""):
        assert self.off + ncols <= self.cols, \
            f"{self.name} overflow at {tag}: {self.off}+{ncols}>{self.cols}"
        ap = self.t[:, self.off:self.off + ncols]
        self.off += ncols
        return ap

    def push(self):
        self.marks.append(self.off)

    def pop(self):
        self.off = self.marks.pop()


def ts(i, n):
    return slice(i * n, (i + 1) * n)


def build_program(debug=(), parts=("proj", "shared", "moe", "resid")):
    nc = bass.Bass("TRN2", target_bir_lowering=False)
    dbg = set(debug)

    def din(name, shape, dt=F32):
        return nc.dram_tensor(name, list(shape), dt, kind="ExternalInput").ap()

    xT = din("xT", (D, SEG))
    xnat = din("xnat", (SEG, D))
    fxT = din("fxT", (D, FOR))
    fxn = din("fxn", (FOR, D))
    fmask_cd = din("fmask_c", (CH, NFC))
    wqkvT = din("wqkvT", (D, 3 * D))
    bqkv = din("bqkv", (1, 3 * D))
    bqkv_cd = din("bqkv_cols", (CH, 48))
    woutT = din("woutT", (D, D))
    bout = din("bout", (1, D))
    shupT = din("shupT", (D, I2))
    shdownT = din("shdownT", (I2, D))
    gateT = din("gateT", (D, E))
    gbias = din("gbias", (1, E))
    w1T = din("w1T", (E, D, I1))
    w2T = din("w2T", (E, I1, D))
    dmaskT_d = din("dmaskT", (CH, H * CH))
    dqtab_d = din("dqtab", (CH, H * CH))
    dktab_d = din("dktab", (CH, H * CH))
    dkcols_d = din("dkcols", (CH, H))
    lamc_d = din("lamc_cols", (CH, H))
    fconst_d = din("fconst", (CH, 8))
    idnbf_d = din("idn_bf", (CH, CH))
    idnf_d = din("idn_f", (CH, CH))

    out_d = nc.dram_tensor("out", [SEG, D], F32, kind="ExternalOutput").ap()
    dbg_out = {}
    for nm, shape in (("d_xnT", (D, SEG)), ("d_wd", (SEG, E)), ("d_sw", (E, SEG)),
                      ("d_spref", (CH, H * HD)), ("d_attnT", (D, SEG)),
                      ("d_stat", (CH, 896)), ("d_rows", (1, 3 * SEG)), ("d_mB", (CH, D)),
                      ("d_rB", (CH, D)), ("d_logit", (SEG, E)), ("d_route", (CH, 512)),
                      ("d_q", (CH, SEG)), ("d_k", (CH, SEG)), ("d_v", (CH, SEG)),
                      ("d_hs0", (I1, SEG)), ("d_hs1", (I1, SEG)),
                      ("d_sqw", (CH, SEG)), ("d_sqr", (1, SEG)), ("d_sqd", (E, SEG))):
        if nm in dbg:
            dbg_out[nm] = nc.dram_tensor(nm, list(shape), F32, kind="ExternalOutput").ap()

    hsT_d = nc.dram_tensor("hsT_sc", [E, I1, SEG], BF16, kind="ExternalOutput").ap()
    sqw_d = nc.dram_tensor("sqw_sc", [E, SEG], F32, kind="ExternalOutput").ap()
    rT_d = nc.dram_tensor("rT_sc", [I2, SEG], BF16, kind="ExternalOutput").ap()
    attnT_d = nc.dram_tensor("attnT_sc", [D, SEG], BF16, kind="ExternalOutput").ap()

    from contextlib import ExitStack
    ctxmgr = ExitStack()

    def sbt(name, shape, dt):
        return ctxmgr.enter_context(nc.sbuf_tensor(name, list(shape), dt))

    AR_BF_COLS = 47104
    AR_F_COLS = 16384
    arbf_t = sbt("arbf", (CH, AR_BF_COLS), BF16)
    arf_t = sbt("arf", (CH, AR_F_COLS), F32)
    sqrtwT = sbt("sqrtwT", (E, SEG), F32)
    dmaskT = sbt("dmaskT_s", (CH, H * CH), BF16)
    dqtab = sbt("dqtab_s", (CH, H * CH), BF16)
    dkcols = sbt("dkcols_s", (CH, H), F32)
    lamc = sbt("lamc_s", (CH, H), F32)
    fconst = sbt("fconst_s", (CH, 8), F32)
    fmask_c = sbt("fmask_cs", (CH, NFC), F32)
    idn_bf = sbt("idn_bf_s", (CH, CH), BF16)
    idn_f = sbt("idn_f_s", (CH, CH), F32)
    ones_bf = sbt("ones_bf", (1, 512), BF16)
    onescol_bf = sbt("onescol_bf", (CH, 1), BF16)
    s_pref = sbt("s_pref", (CH, H * HD), F32)
    bqkv_bf = sbt("bqkv_bf", (1, 3 * D), BF16)
    bout_bf = sbt("bout_bf", (1, D), BF16)
    gb_row = sbt("gb_row", (1, E), F32)
    gbB = sbt("gbB", (CH, E), F32)
    ninf = sbt("ninf", (CH, E), F32)
    onesrow_f = sbt("onesrow_f", (1, CH), F32)
    statc = sbt("statc", (CH, 896), F32)
    gateT_sb = sbt("gateT_sb", (CH, H * E), F32)
    gm64_u = sbt("gm64_u", (CH, E), U32)
    bqkv_cols = sbt("bqkv_cols_s", (CH, 48), F32)
    sel_u = sbt("sel_u", (CH, E), U32)

    PA = ctxmgr.enter_context(nc.psum_tensor("PA", [CH, 2048], F32))
    PB = ctxmgr.enter_context(nc.psum_tensor("PB", [CH, 1024], F32))
    PC = ctxmgr.enter_context(nc.psum_tensor("PC", [CH, 512], F32))
    PT = ctxmgr.enter_context(nc.psum_tensor("PT", [CH, 1024], BF16))

    semnames = ["ld", "cast", "st", "stx", "pe", "pe2", "dve", "dve2", "act",
                "pool", "ld2", "ld3"]
    semh = {s: ctxmgr.enter_context(nc.semaphore(s)) for s in semnames}

    def emit(ctx: Ctx):
        arbf = Arena(arbf_t, AR_BF_COLS, "arbf")
        arf = Arena(arf_t, AR_F_COLS, "arf")
        xnT = arbf_t[:, 0:H * SEG]

        ctx.dma("SP", dkcols[:], dkcols_d[:], "ld")
        ctx.dma("SP", lamc[:], lamc_d[:], "ld")
        ctx.dma("SP", fconst[:], fconst_d[:], "ld")
        ctx.dma("SP", fmask_c[:], fmask_cd[:], "ld")
        ctx.dma("SP", idn_f[:], idnf_d[:], "ld")
        ctx.dma("SP", bqkv_cols[:], bqkv_cd[:], "ld")
        ctx.dma("SP", gb_row[:], gbias[:], "ld")
        ctx.dma("SP", gateT_sb[:].rearrange("p (c e) -> p c e", e=E),
                gateT.rearrange("(c p) e -> p c e", p=CH), "ld")
        ctx.dma("POOL", dmaskT[:], dmaskT_d[:], "cast")
        ctx.dma("POOL", dqtab[:], dqtab_d[:], "cast")
        ctx.dma("POOL", idn_bf[:], idnbf_d[:], "cast")
        ctx.dma("POOL", bqkv_bf[:], bqkv[:], "cast")
        ctx.dma("POOL", bout_bf[:], bout[:], "cast")
        ctx.op("DVE", lambda e: e.memset(ones_bf[:], 1.0), incs=(("dve", 1),))
        ctx.op("DVE", lambda e: e.memset(onescol_bf[:], 1.0), incs=(("dve", 1),))
        ctx.op("DVE", lambda e: e.memset(ninf[:], -1e30), incs=(("dve", 1),))
        ctx.op("DVE", lambda e: e.memset(onesrow_f[:], 1.0), incs=(("dve", 1),))
        ctx.op("PE", lambda e: e.matmul(PC[:, 0:E], onesrow_f[:], gb_row[:],
                                        start=True, stop=True),
               waits=(("ld", None), ("dve", None)), incs=(("pe", 1),))
        ctx.op("DVE", lambda e: e.tensor_copy(gbB[:], PC[:, 0:E]),
               waits=(("pe", None),), incs=(("dve", 1),))

        arbf.push()
        arf.push()
        fb = arf.alloc(2 * D, "p1_fb")
        fscr = arf.alloc(D, "p1_scr")
        dktab_a = arf.alloc(H * CH, "p1_dktab")
        dktz = arf.alloc(8 * CH, "p1_dktz")
        spg = arf.alloc(2 * H * CH, "p1_sping")
        wkv = arbf.alloc(H * 2048, "p1_wkv")
        kvb = arbf.alloc(2 * 2048, "p1_kv")
        kdk = arbf.alloc(8 * CH, "p1_kdk")
        btab = arbf.alloc(2048, "p1_btab")
        wsum = arbf.alloc(2048, "p1_wsum")
        mrow = arbf.alloc(CH, "p1_mrow")
        ftb = arbf.alloc(2 * H * CH, "p1_ft")
        ctx.dma("SP", dktab_a[:], dktab_d[:], "ld")
        fs_m = statc[:, 200:248]
        fs_q = statc[:, 248:296]
        fs_r = statc[:, 296:344]
        for fc in range(NFC):
            bufp = fb[:, ts(fc % 2, D)]
            ctx.dma("SP", bufp, fxn[ts(fc, CH), :], "ld",
                    waits=(("dve", None), ("act", None)) if fc >= 2 else ())
            ctx.op("DVE", lambda e, b=bufp, fc=fc: e.reduce_sum(
                fs_m[:, fc:fc + 1], b, axis=AXX, negate=True),
                waits=(("ld", None),), incs=(("dve", 1),))
            ctx.op("ACT", lambda e, b=bufp, fc=fc: e.activation(
                fscr[:], b, AF.Square, accum_out=fs_q[:, fc:fc + 1]),
                waits=(("ld", None),), incs=(("act", 1),))
        ctx.op("DVE", lambda e: e.tensor_single_scalar(fs_m[:], fs_m[:], fconst[:, 3:4], ALU.mult),
               waits=(("dve", None),), incs=(("dve", 1),))
        ctx.op("DVE", lambda e: e.tensor_single_scalar(fs_q[:], fs_q[:], fconst[:, 3:4], ALU.mult),
               waits=(("dve", None), ("act", None)), incs=(("dve", 1),))
        ctx.op("DVE", lambda e: e.tensor_mul(fs_r[:], fs_m[:], fs_m[:]),
               waits=(("dve", None),), incs=(("dve", 1),))
        ctx.op("DVE", lambda e: e.tensor_sub(fs_r[:], fs_q[:], fs_r[:]),
               waits=(("dve", None),), incs=(("dve", 1),))
        ctx.op("ACT", lambda e: e.activation(fs_r[:], fs_r[:], AF.Sqrt, bias=fconst[:, 0:1]),
               waits=(("dve", None),), incs=(("act", 1),))
        ctx.op("DVE", lambda e: e.reciprocal(fs_r[:], fs_r[:]),
               waits=(("act", None),), incs=(("dve", 1),))
        ctx.op("DVE", lambda e: e.memset(spg[:], 0.0), incs=(("dve", 1),))
        for half in range(2):
            hh0 = half * 8
            for kv_i, base in ((0, D), (1, 2 * D)):
                ctx.dma("POOL",
                        wkv[:].rearrange("p (c f) -> p c f", f=2048)[:, :, ts(kv_i, 1024)],
                        wqkvT.rearrange("(c p) f -> p c f", p=CH)[:, :, base + hh0 * HD:base + (hh0 + 8) * HD],
                        "cast", waits=((("pe", None),) if half == 1 else ()))
            cast_w = ctx.val("cast")
            for q in range(4):
                for hc in range(H):
                    ctx.op("PE", lambda e, q=q, hc=hc: e.matmul(
                        PC[0:1, 0:512], onescol_bf[:],
                        wkv[:, hc * 2048 + q * 512:hc * 2048 + q * 512 + 512],
                        start=(hc == 0), stop=(hc == H - 1)),
                        waits=(("cast", cast_w), ("dve", None)), incs=(("pe", 1),))
                ctx.op("DVE", lambda e, q=q: e.tensor_copy(wsum[0:1, ts(q, 512)], PC[0:1, 0:512]),
                       waits=(("pe", None),), incs=(("dve", 1),))
            for q in range(4):
                base = D + hh0 * HD if q < 2 else 2 * D + hh0 * HD - 1024
                ctx.op("PE", lambda e, q=q, base=base: e.matmul(
                    PC[:, 0:512], ones_bf[0:1, 0:CH],
                    bqkv_bf[0:1, base + q * 512:base + q * 512 + 512],
                    start=True, stop=True),
                    waits=(("dve", None), ("cast", cast_w)), incs=(("pe", 1),))
                ctx.op("DVE", lambda e, q=q: e.tensor_copy(btab[:, ts(q, 512)], PC[:, 0:512]),
                       waits=(("pe", None),), incs=(("dve", 1),))
            for fc in range(NFC):
                ftile = ftb[:, ts(fc % 2, H * CH)]
                ctx.dma("POOL", ftile.rearrange("p (c t) -> p c t", t=CH),
                        fxT[:, ts(fc, CH)].rearrange("(c p) t -> p c t", p=CH),
                        "cast", waits=(("pe", None),))
                ctx.op("PE", lambda e, fc=fc: e.transpose(PC[0:1, 0:CH], fs_m[:, fc:fc + 1], idn_f[:]),
                       waits=(("dve", None),), incs=(("pe", 1),))
                ctx.op("DVE", lambda e: e.tensor_copy(mrow[0:1, :], PC[0:1, 0:CH]),
                       waits=(("pe", None),), incs=(("dve", 1),))
                kvt = kvb[:, ts(fc % 2, 2048)]
                for fbk in range(4):
                    pbs = PB[:, ts(fbk % 2, 512)]
                    ctx.op("PE", lambda e, fbk=fbk, pbs=pbs: e.matmul(
                        pbs, mrow[0:1, :], wsum[0:1, ts(fbk, 512)], start=True, stop=False),
                        waits=(("cast", None), ("dve", None)), incs=(("pe", 1),))
                    for hc in range(H):
                        ctx.op("PE", lambda e, fbk=fbk, hc=hc, pbs=pbs, ftile=ftile: e.matmul(
                            pbs, ftile[:, ts(hc, CH)],
                            wkv[:, hc * 2048 + fbk * 512:hc * 2048 + fbk * 512 + 512],
                            start=False, stop=(hc == H - 1)),
                            incs=(("pe", 1),))
                    ctx.op("DVE", lambda e, fbk=fbk, pbs=pbs, kvt=kvt, fc=fc: e.scalar_tensor_tensor(
                        kvt[:, ts(fbk, 512)], pbs, fs_r[:, fc:fc + 1], btab[:, ts(fbk, 512)],
                        ALU.mult, ALU.add), waits=(("pe", None),), incs=(("dve", 1),))
                ctx.op("DVE", lambda e, fc=fc: e.tensor_single_scalar(
                    dktz[:], dktab_a[:, hh0 * CH:(hh0 + 8) * CH], fmask_c[:, fc:fc + 1], ALU.mult),
                    waits=(("dve", None),), incs=(("dve", 1),))
                ctx.op("DVE", lambda e, kvt=kvt: e.tensor_mul(kdk[:], kvt[:, 0:1024], dktz[:]),
                       waits=(("dve", None),), incs=(("dve", 1),))
                pe_w = ctx.val("dve")
                for h8 in range(8):
                    ctx.op("PE", lambda e, h8=h8, kvt=kvt: e.matmul(
                        PA[:, ts(h8, CH)], kdk[:, ts(h8, CH)], kvt[:, 1024 + h8 * CH:1024 + (h8 + 1) * CH],
                        start=True, stop=True),
                        waits=(("dve", pe_w),), incs=(("pe", 1),))
                p_in, p_out = fc % 2, 1 - fc % 2
                for h8 in range(8):
                    hcol = hh0 + h8
                    ctx.op("DVE", lambda e, h8=h8, hcol=hcol, p_in=p_in, p_out=p_out: e.scalar_tensor_tensor(
                        spg[:, p_out * 2048 + hcol * CH:p_out * 2048 + (hcol + 1) * CH],
                        spg[:, p_in * 2048 + hcol * CH:p_in * 2048 + (hcol + 1) * CH],
                        lamc[:, hcol:hcol + 1], PA[:, ts(h8, CH)], ALU.mult, ALU.add),
                        waits=(("pe", None),), incs=(("dve", 1),))
            pf = 1 - (NFC - 1) % 2
            ctx.op("DVE", lambda e, pf=pf, hh0=hh0: e.tensor_copy(
                s_pref[:, hh0 * CH:(hh0 + 8) * CH], spg[:, pf * 2048 + hh0 * CH:pf * 2048 + (hh0 + 8) * CH]),
                waits=(("dve", None),), incs=(("dve", 1),))
        if "d_spref" in dbg_out:
            ctx.dma("SP", dbg_out["d_spref"][:], s_pref[:], "stx", waits=(("dve", None),))
        arf.pop()
        arbf.pop()

        arf.push()
        xa = arf.alloc(2 * D, "xa")
        xnf = arf.alloc(D, "xnf")
        scr = xnf
        mB = arf.alloc(D, "mB")
        rB = arf.alloc(D, "rB")
        arf.push()
        rowa = arf.alloc(3 * SEG, "rows")
        m_row = rowa[0:1, 0:SEG]
        sq_row = rowa[0:1, SEG:2 * SEG]
        rr_row = rowa[0:1, 2 * SEG:3 * SEG]
        mcols = statc[:, 0:NTC]
        sqcols = statc[:, NTC:2 * NTC]
        for j in range(NTC):
            buf = xa[:, ts(j % 2, D)]
            ctx.dma("SP", buf, xnat[ts(j, CH), :], "ld",
                    waits=(("dve", None), ("act", None)))
            ctx.op("DVE", lambda e, b=buf, j=j: e.reduce_sum(mcols[:, j:j + 1], b, axis=AXX),
                   waits=(("ld", None),), incs=(("dve", 1),))
            ctx.op("ACT", lambda e, b=buf, j=j: e.activation(
                scr[:], b, AF.Square, accum_out=sqcols[:, j:j + 1]),
                waits=(("ld", None),), incs=(("act", 1),))
        for j in range(NTC):
            ctx.op("PE", lambda e, j=j: e.transpose(PC[0:1, 0:CH], mcols[:, j:j + 1], idn_f[:]),
                   waits=(("dve", None), ("act", None)), incs=(("pe", 1),))
            ctx.op("DVE", lambda e, j=j: e.tensor_copy(m_row[:, ts(j, CH)], PC[0:1, 0:CH]),
                   waits=(("pe", None),), incs=(("dve", 1),))
            ctx.op("PE", lambda e, j=j: e.transpose(PC[0:1, 0:CH], sqcols[:, j:j + 1], idn_f[:]),
                   waits=(("dve", None),), incs=(("pe", 1),))
            ctx.op("DVE", lambda e, j=j: e.tensor_copy(sq_row[:, ts(j, CH)], PC[0:1, 0:CH]),
                   waits=(("pe", None),), incs=(("dve", 1),))
        ctx.op("DVE", lambda e: e.tensor_single_scalar(m_row, m_row, fconst[0:1, 3:4], ALU.mult),
               waits=(("dve", None),), incs=(("dve", 1),))
        ctx.op("DVE", lambda e: e.tensor_single_scalar(sq_row, sq_row, fconst[0:1, 3:4], ALU.mult),
               waits=(("dve", None),), incs=(("dve", 1),))
        ctx.op("DVE", lambda e: e.tensor_mul(rr_row, m_row, m_row),
               waits=(("dve", None),), incs=(("dve", 1),))
        ctx.op("DVE", lambda e: e.tensor_sub(rr_row, sq_row, rr_row),
               waits=(("dve", None),), incs=(("dve", 1),))
        ctx.op("ACT", lambda e: e.activation(rr_row, rr_row, AF.Sqrt, bias=fconst[0:1, 0:1]),
               waits=(("dve", None),), incs=(("act", 1),))
        ctx.op("DVE", lambda e: e.reciprocal(rr_row, rr_row),
               waits=(("act", None),), incs=(("dve", 1),))
        if "d_stat" in dbg_out:
            ctx.wait("SP", "dve")
            ctx.wait("SP", "act")
            ctx.dma("SP", dbg_out["d_stat"][:], statc[:], "stx")
            ctx.dma("SP", dbg_out["d_rows"][:], rowa[0:1, :], "stx")
        for src_, dst in ((m_row, mB), (rr_row, rB)):
            for q in range(4):
                ctx.op("PE", lambda e, q=q, src_=src_: e.matmul(
                    PA[:, ts(q, 512)], onesrow_f[:], src_[:, ts(q, 512)],
                    start=True, stop=True), waits=(("dve", None),), incs=(("pe", 1),))
            ctx.op("DVE", lambda e, dst=dst: e.tensor_copy(dst[:], PA[:, 0:D]),
                   waits=(("pe", None),), incs=(("dve", 1),))
        if "d_mB" in dbg_out:
            ctx.wait("SP", "dve")
            ctx.dma("SP", dbg_out["d_mB"][:], mB[:], "stx")
            ctx.dma("SP", dbg_out["d_rB"][:], rB[:], "stx")
        for j in range(NTC):
            buf = xa[:, ts(j % 2, D)]
            xb = xnf
            ctx.dma("SP", buf, xT[ts(j, CH), :], "ld2",
                    waits=(("dve", None), ("act", None)))
            ctx.op("DVE", lambda e, b=buf, xb=xb: e.tensor_sub(xb, b, mB[:]),
                   waits=((("ld2", None), ("pe2", None), ("pe", None), ("act", None))
                          if j == 0 else (("ld2", None), ("pe2", None))),
                   incs=(("dve", 1),))
            ctx.op("DVE", lambda e, xb=xb: e.tensor_mul(xb, xb, rB[:]),
                   waits=(("dve", None),), incs=(("dve", 1),))
            ctx.op("DVE", lambda e, xb=xb, j=j: e.tensor_copy(xnT[:, ts(j, SEG)], xb),
                   waits=(("dve", None),), incs=(("dve", 1),))
            for nb in range(4):
                ctx.op("PE", lambda e, xb=xb, j=j, nb=nb: e.matmul(
                    PA[0:E, ts(nb, 512)], gateT_sb[:, ts(j, E)], xb[:, ts(nb, 512)],
                    start=(j == 0), stop=(j == NTC - 1)),
                    waits=(("dve", None),), incs=(("pe2", 1),))
        arf.pop()
        lgT = arf.alloc(SEG, "lgT")
        ctx.op("DVE", lambda e: e.tensor_copy(lgT[0:E, :], PA[0:E, :]),
               waits=(("pe2", None),), incs=(("dve", 1),))
        ro = Arena(statc[:, 128:], 768, "routing")
        sc_f = ro.alloc(E, "sc_f")
        sb_f = ro.alloc(E, "sb_f")
        t8g8 = ro.alloc(E, "t8g8")
        gsc = ro.alloc(G, "gsc")
        t8G = ro.alloc(8, "t8G")
        msk = ro.alloc(E, "msk")
        t8E = ro.alloc(8, "t8E")
        topw = ro.alloc(E, "topw")
        ssum = ro.alloc(4, "ssum")
        wd = ro.alloc(E, "wd")
        swt = ro.alloc(E, "swt")
        sel_f = ro.alloc(E, "sel_f")
        for tc in range(NTC):
            ctx.op("PE", lambda e, tc=tc: e.transpose(PC[:, 0:E], lgT[0:E, ts(tc, CH)],
                                                      idn_f[0:E, 0:E]),
                   waits=(("dve", None), ("act", None), ("stx", None)), incs=(("pe", 1),))
            ctx.op("ACT", lambda e, tc=tc: e.activation(sc_f, PC[:, 0:E], AF.Sigmoid),
                   waits=(("pe", None),), incs=(("act", 1),))
            ctx.op("DVE", lambda e: e.tensor_add(sb_f, sc_f, gbB[:]),
                   waits=(("act", None),), incs=(("dve", 1),))
            for g in range(G):
                ctx.op("DVE", lambda e, g=g: e.max(out=t8g8[:, ts(g, 8)], in_=sb_f[:, ts(g, 8)]),
                       waits=(("dve", None),), incs=(("dve", 1),))
            ctx.op("DVE", lambda e: e.tensor_add(gsc, t8g8[:, 0::8], t8g8[:, 1::8]),
                   waits=(("dve", None),), incs=(("dve", 1),))
            ctx.op("DVE", lambda e: e.max(out=t8G, in_=gsc),
                   waits=(("dve", None),), incs=(("dve", 1),))
            for g in range(G):
                ctx.op("DVE", lambda e, g=g: e.tensor_single_scalar(
                    gm64_u[:, ts(g, 8)], gsc[:, g:g + 1].broadcast_to([CH, 8]),
                    t8G[:, TOPG - 1:TOPG], ALU.is_ge), waits=(("dve", None),), incs=(("dve", 1),))
            ctx.op("DVE", lambda e: e.select(msk, gm64_u[:], sb_f, ninf[:]),
                   waits=(("dve", None),), incs=(("dve", 1),))
            ctx.op("DVE", lambda e: e.max(out=t8E, in_=msk),
                   waits=(("dve", None),), incs=(("dve", 1),))
            ctx.op("DVE", lambda e: e.tensor_single_scalar(sel_u[:], msk, t8E[:, TOPK - 1:TOPK], ALU.is_ge),
                   waits=(("dve", None),), incs=(("dve", 1),))
            ctx.op("DVE", lambda e: e.tensor_copy(sel_f, sel_u[:]),
                   waits=(("dve", None),), incs=(("dve", 1),))
            ctx.op("DVE", lambda e: e.tensor_mul(topw, sc_f, sel_f),
                   waits=(("dve", None),), incs=(("dve", 1),))
            ctx.op("DVE", lambda e: e.reduce_sum(ssum[:, 0:1], topw, axis=AXX),
                   waits=(("dve", None),), incs=(("dve", 1),))
            ctx.op("DVE", lambda e: e.tensor_single_scalar(ssum[:, 1:2], ssum[:, 0:1], fconst[:, 1:2], ALU.add),
                   waits=(("dve", None),), incs=(("dve", 1),))
            ctx.op("DVE", lambda e: e.reciprocal(ssum[:, 2:3], ssum[:, 1:2]),
                   waits=(("dve", None),), incs=(("dve", 1),))
            ctx.op("DVE", lambda e: e.tensor_single_scalar(ssum[:, 3:4], ssum[:, 2:3], fconst[:, 2:3], ALU.mult),
                   waits=(("dve", None),), incs=(("dve", 1),))
            ctx.op("DVE", lambda e: e.tensor_single_scalar(wd, topw, ssum[:, 3:4], ALU.mult),
                   waits=(("dve", None),), incs=(("dve", 1),))
            ctx.op("ACT", lambda e: e.activation(swt, wd, AF.Sqrt),
                   waits=(("dve", None),), incs=(("act", 1),))
            ctx.op("PE", lambda e: e.transpose(PB[0:E, 0:CH], swt, idn_f[:]),
                   waits=(("act", None), ("dve", None)), incs=(("pe", 1),))
            ctx.op("DVE", lambda e, tc=tc: e.tensor_copy(sqrtwT[:, ts(tc, CH)], PB[0:E, 0:CH]),
                   waits=(("pe", None),), incs=(("dve", 1),))
            if "d_wd" in dbg_out:
                ctx.dma("SP", dbg_out["d_wd"][ts(tc, CH), :], wd, "stx",
                        waits=(("dve", None),))
            if tc == 0 and "d_route" in dbg_out:
                rstage = xnf[:, 0:512]
                for k, (src_ap, w_) in enumerate(((sc_f, E), (sb_f, E), (gsc, 8), (t8G, 8),
                                                  (msk, E), (sel_f, E), (topw, E), (ssum, 4),
                                                  (wd, E))):
                    off = [0, 64, 128, 136, 144, 208, 272, 336, 340][k]
                    ctx.op("DVE", lambda e, s=src_ap, o=off, w=w_: e.tensor_copy(
                        rstage[:, o:o + w], s[:, 0:w]), incs=(("dve", 1),))
                ctx.dma("SP", dbg_out["d_route"][:, :], rstage, "stx", waits=(("dve", None),))
        if "d_logit" in dbg_out:
            for tc in range(NTC):
                ctx.op("PE", lambda e, tc=tc: e.transpose(PC[:, 0:E], lgT[0:E, ts(tc, CH)],
                                                          idn_f[0:E, 0:E]),
                       waits=(("dve", None), ("stx", None)), incs=(("pe", 1),))
                ctx.op("DVE", lambda e, tc=tc: e.tensor_copy(statc[:, 128:128 + E], PC[:, 0:E]),
                       waits=(("pe", None), ("stx", None)), incs=(("dve", 1),))
                ctx.dma("SP", dbg_out["d_logit"][ts(tc, CH), :], statc[:, 128:128 + E], "stx",
                        waits=(("dve", None),))
        ctx.dma("SP", sqw_d[:, :], sqrtwT[:], "st", waits=(("dve", None),))
        if "d_xnT" in dbg_out:
            dscr = xa[:, 0:SEG]
            for j in range(NTC):
                ctx.op("DVE", lambda e, j=j: e.tensor_copy(dscr[:], xnT[:, ts(j, SEG)]),
                       waits=(("dve", None), ("stx", None)), incs=(("dve", 1),))
                ctx.dma("SP", dbg_out["d_xnT"][ts(j, CH), :], dscr[:], "stx",
                        waits=(("dve", None),))
        if "d_sw" in dbg_out:
            dscr2 = xa[:, 0:SEG]
            ctx.op("DVE", lambda e: e.tensor_copy(dscr2[0:E, :], sqrtwT[:]),
                   waits=(("dve", None), ("stx", None)), incs=(("dve", 1),))
            ctx.dma("SP", dbg_out["d_sw"][:], dscr2[0:E, :], "stx", waits=(("dve", None),))
        arf.pop()

        arbf.push()
        arbf.off = H * SEG
        arf.push()
        wq_sb = arbf.alloc(H * CH, "p2_wq")
        wk_sb = arbf.alloc(H * CH, "p2_wk")
        qT = arbf.alloc(SEG, "p2_qT")
        kT = arbf.alloc(SEG, "p2_kT")
        vv = arbf.alloc(SEG, "p2_v")
        ahead = arbf.alloc(SEG, "p2_ahead")
        st_b = arbf.alloc(CH, "p2_st")
        kdk2 = arbf.alloc(CH, "p2_kdk")
        qdq = arbf.alloc(CH, "p2_qdq")
        sbf = arbf.alloc(CH, "p2_sbf")
        sf2 = arf.alloc(2 * CH, "p2_sf")
        for h in range(H):
            ctx.dma("POOL", wq_sb[:].rearrange("p (c t) -> p c t", t=CH),
                    wqkvT[:, ts(h, HD)].rearrange("(c p) t -> p c t", p=CH),
                    "cast", waits=(("pe", None),))
            ctx.dma("POOL", wk_sb[:].rearrange("p (c t) -> p c t", t=CH),
                    wqkvT[:, D + h * HD:D + (h + 1) * HD].rearrange("(c p) t -> p c t", p=CH),
                    "cast")
            cw = ctx.val("cast")
            for name, wsb, boff, dstT in (("q", wq_sb, h * HD, qT), ("k", wk_sb, D + h * HD, kT)):
                for hc in range(H):
                    for tb in range(4):
                        ctx.op("PE", lambda e, wsb=wsb, hc=hc, tb=tb: e.matmul(
                            PA[:, ts(tb, 512)], wsb[:, ts(hc, CH)],
                            xnT[:, hc * SEG + tb * 512:hc * SEG + (tb + 1) * 512],
                            start=(hc == 0), stop=False),
                            waits=(("cast", cw), ("dve", None)), incs=(("pe", 1),))
                for tb in range(4):
                    ctx.op("PE", lambda e, boff=boff, tb=tb: e.matmul(
                        PA[:, ts(tb, 512)], bqkv_bf[0:1, boff:boff + HD],
                        ones_bf[0:1, 0:512], start=False, stop=True),
                        incs=(("pe", 1),))
                for tb in range(4):
                    ctx.op("DVE", lambda e, dstT=dstT, tb=tb: e.tensor_copy(
                        dstT[:, ts(tb, 512)], PA[:, ts(tb, 512)]),
                        waits=(("pe", None),), incs=(("dve", 1),))
            ctx.dma("POOL", wq_sb[:].rearrange("p (c t) -> p c t", t=CH),
                    wqkvT[:, 2 * D + h * HD:2 * D + (h + 1) * HD].rearrange("(c p) t -> p c t", p=CH),
                    "cast", waits=(("pe", None),))
            cw2 = ctx.val("cast")
            for hc in range(H):
                for tb in range(4):
                    ctx.op("PE", lambda e, hc=hc, tb=tb: e.matmul(
                        PA[:, ts(tb, 512)], wq_sb[:, ts(hc, CH)],
                        xnT[:, hc * SEG + tb * 512:hc * SEG + (tb + 1) * 512],
                        start=(hc == 0), stop=(hc == H - 1)),
                        waits=(("cast", cw2), ("dve", None)), incs=(("pe", 1),))
            for tb in range(4):
                ctx.op("DVE", lambda e, tb=tb, h=h: e.tensor_scalar(
                    ahead[:, ts(tb, 512)], PA[:, ts(tb, 512)],
                    bqkv_cols[:, 32 + h:33 + h], None, ALU.add),
                    waits=(("pe", None), ("st", None)), incs=(("dve", 1),))
            for c in range(NTC):
                ctx.op("PE", lambda e, c=c: e.transpose(PT[:, 0:CH], ahead[:, ts(c, CH)], idn_bf[:]),
                       waits=(("dve", None),), incs=(("pe", 1),))
                ctx.op("DVE", lambda e, c=c: e.tensor_copy(vv[:, ts(c, CH)], PT[:, 0:CH]),
                       waits=(("pe", None),), incs=(("dve", 1),))
            if h == 15 and "d_q" in dbg_out:
                for nm, tile in (("d_q", qT), ("d_k", kT), ("d_v", vv)):
                    ctx.op("DVE", lambda e, t=tile: e.tensor_copy(xnf, t[:]),
                           waits=(("dve", None), ("stx", None)), incs=(("dve", 1),))
                    ctx.dma("SP", dbg_out[nm][:], xnf, "stx", waits=(("dve", None),))
            ctx.op("DVE", lambda e, h=h: e.tensor_copy(sf2[:, 0:CH], s_pref[:, ts(h, CH)]),
                   waits=(("dve", None),), incs=(("dve", 1),))
            ctx.op("DVE", lambda e: e.tensor_copy(sbf[:], sf2[:, 0:CH]),
                   waits=(("dve", None),), incs=(("dve", 1),))
            for c in range(NTC):
                p_in, p_out = c % 2, 1 - c % 2
                ctx.op("DVE", lambda e, c=c, h=h: e.tensor_mul(
                    qdq[:], qT[:, ts(c, CH)], dqtab[:, ts(h, CH)]),
                    waits=(("dve", None), ("pe", None)), incs=(("dve", 1),))
                ctx.op("PE", lambda e, c=c: e.matmul(
                    PB[:, 0:CH], kT[:, ts(c, CH)], qT[:, ts(c, CH)], start=True, stop=True),
                    waits=(("dve", None),), incs=(("pe", 1),))
                ctx.op("DVE", lambda e, h=h: e.tensor_mul(st_b[:], PB[:, 0:CH], dmaskT[:, ts(h, CH)]),
                       waits=(("pe", None),), incs=(("dve", 1),))
                ctx.op("PE", lambda e, c=c: e.transpose(PT[:, 0:CH], kT[:, ts(c, CH)], idn_bf[:]),
                       waits=(("dve", None),), incs=(("pe", 1),))
                ctx.op("DVE", lambda e, h=h: e.tensor_scalar(
                    kdk2[:], PT[:, 0:CH], dkcols[:, h:h + 1], None, ALU.mult),
                    waits=(("pe", None),), incs=(("dve", 1),))
                pe_after = ctx.val("dve")
                ctx.op("PE", lambda e, c=c: e.matmul(
                    PB[:, 512:512 + CH], vv[:, ts(c, CH)], st_b[:], start=True, stop=False),
                    waits=(("dve", pe_after),), incs=(("pe", 1),))
                ctx.op("PE", lambda e: e.matmul(
                    PB[:, 512:512 + CH], sbf[:], qdq[:], start=False, stop=True),
                    incs=(("pe", 1),))
                ctx.op("PE", lambda e, c=c: e.matmul(
                    PC[:, 0:CH], kdk2[:], vv[:, ts(c, CH)], start=True, stop=True),
                    incs=(("pe", 1),))
                ctx.op("DVE", lambda e, c=c: e.tensor_copy(ahead[:, ts(c, CH)], PB[:, 512:512 + CH]),
                       waits=(("pe", None),), incs=(("dve", 1),))
                ctx.op("DVE", lambda e, h=h, p_in=p_in, p_out=p_out: e.scalar_tensor_tensor(
                    sf2[:, ts(p_out, CH)], sf2[:, ts(p_in, CH)], lamc[:, h:h + 1],
                    PC[:, 0:CH], ALU.mult, ALU.add),
                    waits=(("dve", None), ("pe", None)), incs=(("dve", 1),))
                ctx.op("DVE", lambda e, p_out=p_out: e.tensor_copy(sbf[:], sf2[:, ts(p_out, CH)]),
                       waits=(("dve", None),), incs=(("dve", 1),))
            ctx.dma("SP", attnT_d[ts(h, CH), :], ahead[:], "st", waits=(("dve", None),))
            if "d_attnT" in dbg_out:
                ctx.op("DVE", lambda e: e.tensor_copy(xnf, ahead[:]),
                       waits=(("dve", None), ("stx", None)), incs=(("dve", 1),))
                ctx.dma("SP", dbg_out["d_attnT"][ts(h, CH), :], xnf, "stx",
                        waits=(("dve", None),))
        arf.pop()
        arbf.pop()

        arbf.push()
        arbf.off = H * SEG
        arf.push()
        w1sb = arbf.alloc(H * I1, "p3_w1")
        hst = arbf.alloc(2 * SEG, "p3_hst")
        sqwb = arf.alloc(SEG, "p3_sqwb")
        sqrow = arf.alloc(SEG, "p3_sqrow")
        tmp1 = arf.alloc(1024, "p3_tmp")
        for e in range(E):
            ctx.dma("POOL", w1sb[:].rearrange("p (c i) -> p c i", i=I1),
                    w1T[e].rearrange("(c p) i -> p c i", p=CH), "cast",
                    waits=(("pe", None),))
            ctx.dma("SP", sqrow[0:1, :], sqw_d[e:e + 1, :], "ld",
                    waits=(("pe", None), ("st", None)))
            cw = ctx.val("cast")
            lw = ctx.val("ld")
            for q in range(4):
                ctx.op("PE", lambda e_, q=q: e_.matmul(
                    PB[:, ts(q % 2, 512)], onesrow_f[:], sqrow[0:1, ts(q, 512)],
                    start=True, stop=True),
                    waits=(("ld", lw), ("dve", None)), incs=(("pe", 1),))
                ctx.op("DVE", lambda e_, q=q: e_.tensor_copy(sqwb[:, ts(q, 512)], PB[:, ts(q % 2, 512)]),
                       waits=(("pe", None),), incs=(("dve", 1),))
            if e == 0 and "d_sqw" in dbg_out:
                ctx.wait("SP", "dve")
                ctx.dma("SP", dbg_out["d_sqw"][:, :], sqwb[:], "stx")
                ctx.dma("SP", dbg_out["d_sqr"][:, :], sqrow[0:1, :], "stx")
                if "d_sqd" in dbg_out:
                    ctx.wait("SP", "st")
                    ctx.dma("SP", dbg_out["d_sqd"][:, :], sqw_d[:, :], "stx")
            for ic in range(4):
                stg = hst[:, ts(ic % 2, SEG)]
                for hc in range(H):
                    for tb in range(4):
                        ctx.op("PE", lambda e_, hc=hc, tb=tb, ic=ic: e_.matmul(
                            PA[:, ts(tb, 512)],
                            w1sb[:, hc * I1 + ic * CH:hc * I1 + ic * CH + CH],
                            xnT[:, hc * SEG + tb * 512:hc * SEG + (tb + 1) * 512],
                            start=(hc == 0), stop=(hc == H - 1)),
                            waits=((("cast", cw), ("dve", None)) if (hc == 0) else ()),
                            incs=(("pe", 1),))
                for tb in range(4):
                    tp = tmp1[:, ts(tb % 2, 512)]
                    ctx.op("DVE", lambda e_, tb=tb, tp=tp: e_.scalar_tensor_tensor(
                        tp, PA[:, ts(tb, 512)], 0.0, sqwb[:, ts(tb, 512)],
                        ALU.max, ALU.mult),
                        waits=(("pe", None), ("act", None)), incs=(("dve", 1),))
                    ctx.op("ACT", lambda e_, tb=tb, tp=tp, stg=stg: e_.activation(
                        stg[:, ts(tb, 512)], tp, AF.Square),
                        waits=(("dve", None), ("st", None)), incs=(("act", 1),))
                ctx.dma("SP", hsT_d[e, ts(ic, CH), :], stg, "st",
                        waits=(("act", None),))
        if "d_hs0" in dbg_out:
            ctx.dma("POOL", dbg_out["d_hs0"][:, :], hsT_d[0][:, :], "stx",
                    waits=(("st", None),))
            ctx.dma("POOL", dbg_out["d_hs1"][:, :], hsT_d[63][:, :], "stx",
                    waits=(("st", None),))
        arf.pop()
        arbf.pop()

        arbf.push()
        arbf.off = H * SEG
        arf.push()
        shu = arbf.alloc(2 * SEG, "p4_shu")
        rstg = arbf.alloc(2 * SEG, "p4_rstg")
        tmp2 = arf.alloc(1024, "p4_tmp")
        for i2c in range(H):
            wsl = shu[:, ts(i2c % 2, SEG)]
            ctx.dma("POOL", wsl.rearrange("p (c t) -> p c t", t=CH),
                    shupT[:, ts(i2c, CH)].rearrange("(c p) t -> p c t", p=CH),
                    "cast", waits=(("pe", None),))
            cw = ctx.val("cast")
            stg = rstg[:, ts(i2c % 2, SEG)]
            for hc in range(H):
                for tb in range(4):
                    ctx.op("PE", lambda e_, hc=hc, tb=tb, wsl=wsl: e_.matmul(
                        PA[:, ts(tb, 512)], wsl[:, ts(hc, CH)],
                        xnT[:, hc * SEG + tb * 512:hc * SEG + (tb + 1) * 512],
                        start=(hc == 0), stop=(hc == H - 1)),
                        waits=((("cast", cw), ("dve", None)) if hc == 0 else ()),
                        incs=(("pe", 1),))
            for tb in range(4):
                tp = tmp2[:, ts(tb % 2, 512)]
                ctx.op("DVE", lambda e_, tb=tb, tp=tp: e_.tensor_scalar(
                    tp, PA[:, ts(tb, 512)], 0.0, None, ALU.max),
                    waits=(("pe", None), ("act", None)), incs=(("dve", 1),))
                ctx.op("ACT", lambda e_, tb=tb, tp=tp, stg=stg: e_.activation(
                    stg[:, ts(tb, 512)], tp, AF.Square),
                    waits=(("dve", None), ("st", None)), incs=(("act", 1),))
            ctx.dma("SP", rT_d[ts(i2c, CH), :], stg, "st", waits=(("act", None),))
        arf.pop()
        arbf.pop()

        HT = SEG // 2
        slots = (PA[:, 0:1024], PA[:, 1024:2048], PB[:, 0:1024])
        for half in range(2):
            t0 = half * HT
            arbf.push()
            arf.push()
            acc = arf.alloc(16384, "p5_acc")
            sidx = 0
            written = set()

            def acc_update(tc, dh, slot):
                accsl = acc[:, tc * 2048 + dh * 1024:tc * 2048 + (dh + 1) * 1024]
                if (tc, dh) in written:
                    ctx.op("DVE", lambda e_, accsl=accsl, slot=slot: e_.tensor_tensor(
                        accsl, accsl, slot, ALU.add),
                        waits=(("pe", None),), incs=(("dve", 1),))
                else:
                    written.add((tc, dh))
                    ctx.op("DVE", lambda e_, accsl=accsl, slot=slot: e_.tensor_copy(accsl, slot),
                           waits=(("pe", None),), incs=(("dve", 1),))

            for wdram, srcT, srcname in [(w_, s_, n_) for (w_, s_, n_) in
                                          ((woutT, attnT_d, "proj"), (shdownT, rT_d, "shared"))
                                          if n_ in parts]:
                arbf.push()
                a_sb = arbf.alloc(H * 1024, "p5_a")
                w_sb = arbf.alloc(H * 1024, "p5_w")
                ctx.dma("SP", a_sb[:].rearrange("p (c t) -> p c t", t=1024),
                        srcT[:, t0:t0 + HT].rearrange("(c p) t -> p c t", p=CH),
                        "ld2", waits=(("pe", None), ("st", None)))
                for dh in range(2):
                    ctx.dma("POOL", w_sb[:].rearrange("p (c f) -> p c f", f=1024),
                            wdram[:, dh * 1024:(dh + 1) * 1024].rearrange("(c p) f -> p c f", p=CH),
                            "cast", waits=(("pe", None),))
                    cw = ctx.val("cast")
                    lw = ctx.val("ld2")
                    isb = (srcname == "proj")
                    for tc in range(8):
                        slot = slots[sidx % 3]
                        sidx += 1
                        for nb in range(2):
                            for kc in range(H):
                                ctx.op("PE", lambda e_, kc=kc, tc=tc, nb=nb, slot=slot, a_sb=a_sb, w_sb=w_sb: e_.matmul(
                                    slot[:, ts(nb, 512)],
                                    a_sb[:, kc * 1024 + tc * CH:kc * 1024 + (tc + 1) * CH],
                                    w_sb[:, kc * 1024 + nb * 512:kc * 1024 + nb * 512 + 512],
                                    start=(kc == 0), stop=(kc == H - 1 and not isb)),
                                    waits=((("dve", None), ("cast", cw), ("ld2", lw))
                                           if kc == 0 else ()),
                                    incs=(("pe", 1),))
                            if isb:
                                ctx.op("PE", lambda e_, dh=dh, nb=nb, slot=slot: e_.matmul(
                                    slot[:, ts(nb, 512)], ones_bf[0:1, 0:CH],
                                    bout_bf[0:1, dh * 1024 + nb * 512:dh * 1024 + nb * 512 + 512],
                                    start=False, stop=True),
                                    incs=(("pe", 1),))
                        acc_update(tc, dh, slot)
                arbf.pop()
            arbf.push()
            hs_sb = arbf.alloc(2 * 4 * 1024, "p5_hs")
            w2_sb = arbf.alloc(2 * 4 * SEG, "p5_w2")
            for ms in (range(E // 2) if "moe" in parts else ()):
                for e2 in range(2):
                    ee = 2 * ms + e2
                    ctx.dma("SP", hs_sb[:, e2 * 4096:(e2 + 1) * 4096].rearrange("p (c t) -> p c t", t=1024),
                            hsT_d[ee][:, t0:t0 + HT].rearrange("(c p) t -> p c t", p=CH),
                            "ld2", waits=(("pe", None), ("st", None)))
                    ctx.dma("POOL", w2_sb[:, e2 * 8192:(e2 + 1) * 8192].rearrange("p (c f) -> p c f", f=2048),
                            w2T[ee].rearrange("(c p) f -> p c f", p=CH),
                            "cast", waits=(("pe", None),))
                cw = ctx.val("cast")
                lw = ctx.val("ld2")
                for dh in range(2):
                    for tc in range(8):
                        slot = slots[sidx % 3]
                        sidx += 1
                        for nb in range(2):
                            for ki in range(8):
                                e2, ic = ki // 4, ki % 4
                                ctx.op("PE", lambda e_, e2=e2, ic=ic, tc=tc, dh=dh, nb=nb, slot=slot, ki=ki: e_.matmul(
                                    slot[:, ts(nb, 512)],
                                    hs_sb[:, e2 * 4096 + ic * 1024 + tc * CH:e2 * 4096 + ic * 1024 + (tc + 1) * CH],
                                    w2_sb[:, e2 * 8192 + ic * 2048 + dh * 1024 + nb * 512:
                                          e2 * 8192 + ic * 2048 + dh * 1024 + nb * 512 + 512],
                                    start=(ki == 0), stop=(ki == 7)),
                                    waits=((("dve", None), ("cast", cw), ("ld2", lw))
                                           if ki == 0 and nb == 0 else ()),
                                    incs=(("pe", 1),))
                        acc_update(tc, dh, slots[(sidx - 1) % 3])
            arbf.pop()
            if "resid" in parts:
                ctx.dma("POOL", out_d[t0:t0 + HT, :], xnat[t0:t0 + HT, :], "st",
                        waits=(("dve", None),))
            ctx.wait("POOL", "st")
            ctx.op("POOL", lambda e_: e_.dma_start(
                out=out_d[t0:t0 + HT, :].rearrange("(c p) f -> p c f", p=CH),
                in_=acc[:].rearrange("p (c f) -> p c f", f=2048),
                accum_op=ALU.add), waits=(("dve", None),), incs=(("st", 16),))
            ctx.wait("POOL", "st")
            arf.pop()
            arbf.pop()
        ctx.wait("SP", "st")
        ctx.wait("SP", "stx")
        ctx.wait("SP", "ld")
        ctx.wait("SP", "cast")
        ctx.wait("SP", "ld2")

    with nc.Block() as block:
        def mkpass(name):
            def body(eng):
                c = Ctx({k: (eng if k == name else None) for k in ENGS}, semh)
                c.begin(name)
                emit(c)
            return body
        block.tensor(mkpass("PE"))
        block.vector(mkpass("DVE"))
        block.scalar(mkpass("ACT"))
        block.gpsimd(mkpass("POOL"))
        block.sync(mkpass("SP"))

    return nc, ctxmgr


def _prep_inputs(inputs):
    x = np.asarray(inputs["L0_attn_x"], dtype=np.float32)
    t = _host_tables()
    shared = dict(
        wqkvT=np.ascontiguousarray(np.asarray(inputs["L0_attn_w_qkv"], np.float32).T),
        bqkv=np.asarray(inputs["L0_attn_b_qkv"], np.float32).reshape(1, -1),
        bqkv_cols=np.ascontiguousarray(np.asarray(inputs["L0_attn_b_qkv"], np.float32).reshape(48, 128).T),
        woutT=np.ascontiguousarray(np.asarray(inputs["L0_attn_w_out"], np.float32).T),
        bout=np.asarray(inputs["L0_attn_b_out"], np.float32).reshape(1, -1),
        shupT=np.ascontiguousarray(np.asarray(inputs["L0_ffn_shared_up"], np.float32).T),
        shdownT=np.ascontiguousarray(np.asarray(inputs["L0_ffn_shared_down"], np.float32).T),
        gateT=np.ascontiguousarray(np.asarray(inputs["L0_ffn_gate_w"], np.float32).T),
        gbias=np.asarray(inputs["L0_ffn_gate_bias"], np.float32).reshape(1, -1),
        w1T=np.ascontiguousarray(np.asarray(inputs["L0_ffn_w1"], np.float32).transpose(0, 2, 1)),
        w2T=np.ascontiguousarray(np.asarray(inputs["L0_ffn_w2"], np.float32).transpose(0, 2, 1)),
        dmaskT=t["dmaskT"], dqtab=t["dqtab"], dktab=t["dktab"], dkcols=t["dkcols"],
        lamc_cols=t["lamc_cols"], fconst=t["fconst"], idn_bf=t["idn_bf"], idn_f=t["idn_f"],
    )
    in_maps = []
    for c in range(8):
        b, s = c // 4, c % 4
        seg = x[b, s * SEG:(s + 1) * SEG, :]
        pre = x[b, max(0, s * SEG - FOR):s * SEG, :]
        fx = np.zeros((FOR, D), dtype=np.float32)
        fm = np.zeros((FOR,), dtype=np.float32)
        if pre.shape[0]:
            fx[FOR - pre.shape[0]:] = pre
            fm[FOR - pre.shape[0]:] = 1.0
        m = dict(shared)
        m["xT"] = np.ascontiguousarray(seg.T)
        m["xnat"] = np.ascontiguousarray(seg)
        m["fxT"] = np.ascontiguousarray(fx.T)
        m["fxn"] = fx
        m["fmask_c"] = np.ascontiguousarray(fm.reshape(NFC, CH).T)
        in_maps.append(m)
    return in_maps


_CACHED = {}


def kernel(**inputs):
    if "prog" not in _CACHED:
        _CACHED["prog"] = build_program()
    nc, _ = _CACHED["prog"]
    in_maps = _prep_inputs(inputs)
    res = run_bass_kernel_spmd(nc, in_maps, core_ids=list(range(8)))
    out = np.zeros((B, N, D), dtype=np.float32)
    for c in range(8):
        b, s = c // 4, c % 4
        out[b, s * SEG:(s + 1) * SEG, :] = res.results[c]["out"]
    return out
